# version 31
# baseline (speedup 1.0000x reference)
import numpy as np
import concourse.bass as bass
import concourse.mybir as mybir
from concourse.bass_utils import run_bass_kernel_spmd

F32 = mybir.dt.float32
BF16 = mybir.dt.bfloat16
AF = mybir.ActivationFunctionType
ALU = mybir.AluOpType
AX = mybir.AxisListType.X

GAMMA = 0.99
B, T, D = 4, 8192, 1024
CH = 512
NCH = T // CH
DH = 512
HID = 4096
NHS = 16
_gc64 = np.float64(GAMMA) ** CH
G_C = float(np.float32(_gc64))
G_C_LO = float(np.float32(_gc64 - np.float64(G_C)))


class Prog:
    ENGS = ("T", "V", "S", "Y", "G")

    def __init__(self):
        self.ops = {e: [] for e in self.ENGS}
        self.cnt = {}
        self.rw = {}
        self.owner = {"t": "T", "v": "V", "s": "S"}

    SELF_SEM = {"V": "v", "S": "s"}
    PAGE = 2048

    def _regions(self, ap):
        t = getattr(ap, "tensor", None)
        if t is None:
            return ()
        rng = getattr(t, "manual_sbuf_range", None)
        if rng:
            return [("S", p) for p in range(rng[0] // self.PAGE,
                                            (rng[1] - 1) // self.PAGE + 1)]
        return [("N", t.name)]

    def op(self, eng, emit, waits=(), inc=None, amt=1):
        w = []
        for x in waits:
            if x is None:
                continue
            sem, val = x
            if self.owner.get(sem) == eng:
                continue
            w.append((sem, val))
        own = self.SELF_SEM.get(eng)
        if own is not None:
            hist = self.rw.setdefault(eng, {})
            k = self.cnt.get(own, 0)
            if hasattr(emit, "w"):
                need = 0
                rd = [r for a in emit.r for r in self._regions(a)]
                wr = [r for a in emit.w for r in self._regions(a)]
                for r in rd:
                    e2 = hist.get(r)
                    if e2:
                        need = max(need, e2[0])
                for r in wr:
                    e2 = hist.get(r)
                    if e2:
                        need = max(need, e2[0], e2[1])
                if need > 0:
                    w.append((own, need))
                idx = k + 1
                for r in rd:
                    e2 = hist.setdefault(r, [0, 0])
                    e2[1] = max(e2[1], idx)
                for r in wr:
                    e2 = hist.setdefault(r, [0, 0])
                    e2[0] = idx
                    e2[1] = max(e2[1], idx)
            else:
                if k > 0:
                    w.append((own, k))
        ev = None
        if inc is not None:
            self.cnt[inc] = self.cnt.get(inc, 0) + amt
            ev = (inc, self.cnt[inc])
            self.ops[eng].append((w, emit, (inc, amt)))
        else:
            self.ops[eng].append((w, emit, None))
        return ev

    def now(self, sem):
        return (sem, self.cnt.get(sem, 0))

    def emit(self, eng_key, engine, sems):
        floors = {}
        for waits, emit, inc in self.ops[eng_key]:
            tgt = {}
            for s, v in waits:
                if v > floors.get(s, 0):
                    tgt[s] = max(tgt.get(s, 0), v)
            for s in sorted(tgt):
                engine.wait_ge(sems[s], tgt[s])
                floors[s] = tgt[s]
            inst = emit(engine)
            if inc is not None:
                s, amt = inc
                inst.then_inc(sems[s], amt)


class DQ:

    def __init__(self, P, eng, base, R=4):
        self.P, self.eng = P, eng
        self.names = [f"{base}{i}" for i in range(R)]
        self.i = 0

    def issue(self, emit, waits=()):
        s = self.names[self.i]
        self.i = (self.i + 1) % len(self.names)
        prior = self.P.cnt.get(s, 0)
        w = list(waits)
        if prior:
            w.append((s, prior))
        return self.P.op(self.eng, emit, waits=w, inc=s, amt=16)

    def done(self):
        return [self.P.now(s) for s in self.names]



def _tag(f, writes, reads, scalars=()):
    f.w = [x for x in writes if x is not None]
    f.r = [x for x in reads if x is not None] + \
          [x for x in scalars if hasattr(x, "tensor")]
    return f

def E_mm(out, l, r, st, sp):
    return _tag(lambda e: e.matmul(out, l, r, start=st, stop=sp), [out], [l, r])

def E_tt(out, a, b2, op):
    return _tag(lambda e: e.tensor_tensor(out, a, b2, op), [out], [a, b2])

def E_mmt(out, l, r, st, sp, tp):
    return _tag(lambda e: e.matmul(out, l, r, start=st, stop=sp, tile_position=tp),
                [out], [l, r])

def E_ts(out, a, s1, s2, op0, op1=None):
    if op1 is None:
        return _tag(lambda e: e.tensor_scalar(out, a, s1, None, op0), [out], [a], [s1])
    return _tag(lambda e: e.tensor_scalar(out, a, s1, s2, op0, op1), [out], [a], [s1, s2])

def E_stt(out, a, sc, b2, op0, op1):
    return _tag(lambda e: e.scalar_tensor_tensor(out, a, sc, b2, op0, op1),
                [out], [a, b2], [sc])

def E_cp(out, a):
    return _tag(lambda e: e.copy(out, a), [out], [a])

def E_act(out, a, f, **kw):
    return _tag(lambda e: e.activation(out, a, f, **kw), [out], [a],
                [kw.get("scale"), kw.get("bias")])

def E_tr(out, a, ident):
    return _tag(lambda e: e.transpose(out, a, ident), [out], [a, ident])

def E_red(out, a, op, negate=False, absval=False):
    return _tag(lambda e: e.tensor_reduce(out, a, AX, op, apply_absolute_value=absval,
                                          negate=negate), [out], [a])

def E_ttr(out, a, b2, scale, init, op0, op1, acc):
    return _tag(lambda e: e.tensor_tensor_reduce(out, a, b2, scale, init, op0, op1, acc),
                [out, acc], [a, b2])

def E_memset(ap, val):
    return _tag(lambda e: e.memset(ap, val), [ap], [])

def E_tt2cp(out, a):
    return _tag(lambda e: e.tensor_copy(out, a), [out], [a])

def E_recip(out, a):
    return _tag(lambda e: e.reciprocal(out, a), [out], [a])


def build_nc(dev_phase1_only=False):
    nc = bass.Bass("TRN2", target_bir_lowering=False)

    xfm = nc.dram_tensor("xfm", [D, T], F32, kind="ExternalInput").ap()
    xres = nc.dram_tensor("xres", [4096, D], F32, kind="ExternalInput").ap()
    wq_d = nc.dram_tensor("wq", [D, D], F32, kind="ExternalInput").ap()
    wk_d = nc.dram_tensor("wk", [D, D], F32, kind="ExternalInput").ap()
    wv_d = nc.dram_tensor("wv", [D, DH], F32, kind="ExternalInput").ap()
    wfc_d = nc.dram_tensor("wfc", [HID, D], F32, kind="ExternalInput").ap()
    wpj_d = nc.dram_tensor("wpj", [D, HID], F32, kind="ExternalInput").ap()
    vt_d = nc.dram_tensor("vt_in", [D, 16], F32, kind="ExternalInput").ap()
    ut_d = nc.dram_tensor("ut_in", [16, HID], F32, kind="ExternalInput").ap()
    fbt_d = nc.dram_tensor("fbt_in", [16, 16], F32, kind="ExternalInput").ap()
    gate_d = nc.dram_tensor("gate_in", [128, 1], F32, kind="ExternalInput").ap()
    mask_d = nc.dram_tensor("mask_in", [128, 4, CH], BF16, kind="ExternalInput").ap()
    qsc_d = nc.dram_tensor("qscale", [128, CH], F32, kind="ExternalInput").ap()
    ksc_d = nc.dram_tensor("kscale", [128, CH], F32, kind="ExternalInput").ap()
    qlo_d = nc.dram_tensor("qscale_lo", [128, CH], F32, kind="ExternalInput").ap()
    klo_d = nc.dram_tensor("kscale_lo", [128, CH], F32, kind="ExternalInput").ap()
    id_d = nc.dram_tensor("ident", [128, 128], F32, kind="ExternalInput").ap()
    igc_d = nc.dram_tensor("igc_in", [128, 128], F32, kind="ExternalInput").ap()
    out_d = nc.dram_tensor("out", [4096, D], F32, kind="ExternalOutput").ap()
    scr_kind = "ExternalOutput" if dev_phase1_only else "Internal"
    scr = nc.dram_tensor("oat_scr", [DH, T], BF16, kind=scr_kind).ap()
    wt1_scr = nc.dram_tensor("wt1_scr", [128, 8, HID], BF16, kind="Internal").ap()
    wt2_scr = nc.dram_tensor("wt2_scr", [128, 32, D], BF16, kind="Internal").ap()

    SBB = 16544

    def sb(name, shape, dtype, off):
        return nc.alloc_sbuf_tensor_at(name, shape, dtype, offset=SBB + off).ap()

    wq = sb("wq_sb", [128, 8, D], BF16, 0)
    wk = sb("wk_sb", [128, 8, D], BF16, 16384)
    wv = sb("wv_sb", [128, 8, DH], BF16, 32768)
    mask = sb("mask_sb", [128, 4, CH], BF16, 40960)
    qsc = sb("qsc_sb", [128, CH], F32, 45056)
    ksc = sb("ksc_sb", [128, CH], F32, 47104)
    qlo = sb("qlo_sb", [128, CH], F32, 49152)
    xbf = [sb(f"xbf{i}", [128, 8, CH], BF16, 51200 + i * 8192) for i in range(2)]
    xq = [sb(f"xq{i}_sb", [128, 8, CH], BF16, [67584, 172032][i]) for i in range(2)]
    xk = [sb(f"xk{i}_sb", [128, 8, CH], BF16, [75776, 180224][i]) for i in range(2)]
    qt = sb("qt_sb", [128, 8, CH], BF16, 92160)
    kf = sb("kf_sb", [128, 8, CH], BF16, 100352)
    kt = sb("kt_sb", [128, 32, 128], BF16, 108544)
    vt = sb("vt_sb", [128, 4, DH], BF16, 116736)
    am = sb("am_sb", [128, 4, CH], BF16, 120832)
    sbf = [sb(f"sbf{i}", [128, 8, DH], BF16, 124928 + i * 8192) for i in range(2)]
    oat = [sb(f"oat{i}", [128, 4, CH], BF16, 141312 + i * 4096) for i in range(2)]
    klo = sb("klo_sb", [128, CH], F32, 149504)
    xtA = sb("xtA_sb", [128, 4, CH], F32, 155648)
    xtB = sb("xtB_sb", [128, 4, CH], F32, 163840)
    igc = sb("igc_sb", [128, 128], BF16, 188416)
    tstg = [sb(f"tstg{i}", [128, D], F32, 188672 + i * 4096) for i in range(2)]
    t1b = sb("t1b_sb", [128, D], F32, 196864)
    twb = [sb(f"twb{i}_sb", [128, D], BF16, [200960, 205184][i]) for i in range(2)]
    tsg = [sb(f"tsg{i}_sb", [128, 8, 128], BF16, [203008, 207232][i]) for i in range(2)]
    tcols = sb("tcols_sb", [128, 32], F32, 205056)
    w1t = sb("w1t_sb", [128, 8, HID], BF16, 0)
    w2t = sb("w2t_sb", [128, 32, D], BF16, 65536)
    idn = sb("id_sb", [128, 128], F32, 131072)
    vt2 = sb("vt2_sb", [128, 8, 16], BF16, 131584)
    ut = sb("ut_sb", [48, HID], BF16, 131840)
    fbt = sb("fbt_sb", [16, 16], BF16, 140032)
    gate = sb("gate_sb", [128, 1], F32, 140096)
    lr1 = sb("lr1_sb", [16, 256], BF16, 140128)
    lr2 = sb("lr2_sb", [48, 256], BF16, 140672)
    cols = sb("cols_sb", [128, 10], F32, 141216)

    att = [sb(f"att{i}", [128, 2, D], BF16, 180544 + i * 4096) for i in range(2)]
    h1 = [sb(f"h1_{i}", [128, 2, D], F32, 188736 + i * 8192) for i in range(2)]
    h1fp = [sb("h1f0_sb", [128, 8, 256], BF16, 141632),
            sb("h1f1_sb", [128, 8, 256], BF16, 205184)]
    r2 = sb("r2_sb", [128, 32, 256], BF16, 145728)
    outf = [sb(f"outf{i}", [128, 2, D], F32, 162112 + i * 8192) for i in range(2)]
    rtmp = sb("rtmp_sb", [128, 2, 256], F32, 178496)


    pb = [nc.alloc_psum_tensor(f"pb{i}", [128, 512], F32).ap() for i in range(8)]

    P = Prog()
    QG = DQ(P, "G", "g", 4)
    QY = DQ(P, "Y", "y", 6)
    QS = DQ(P, "Y", "ts", 1)
    ev_bankfree = [None] * 8
    E = {}

    def dma(dst, src):
        return lambda e: e.dma_start(out=dst, in_=src)

    ld_wq = QG.issue(dma(wq, bass.AP(wq_d.tensor, 0, [[D, 128], [128 * D, 8], [1, D]])))
    ld_wk = QG.issue(dma(wk, bass.AP(wk_d.tensor, 0, [[D, 128], [128 * D, 8], [1, D]])))
    ld_wv = QG.issue(dma(wv, bass.AP(wv_d.tensor, 0, [[DH, 128], [128 * DH, 8], [1, DH]])))
    ld_igc = QG.issue(dma(igc, igc_d))
    ld_msk = QY.issue(dma(mask, mask_d))
    ld_qsc = QY.issue(dma(qsc, qsc_d))
    ld_ksc = QY.issue(dma(ksc, ksc_d))
    ld_qlo = QY.issue(dma(qlo, qlo_d))
    ld_klo = QY.issue(dma(klo, klo_d))

    def ld_x(n):
        dst = xbf[n % 2]
        src = bass.AP(xfm.tensor, n * CH, [[T, 128], [128 * T, 8], [1, CH]])
        return dst, src

    E[("lx", 0)] = QG.issue(dma(*ld_x(0)))
    E[("lx", 1)] = QG.issue(dma(*ld_x(1)))

    def emit_xscales(n):
        p = n % 2
        lx = E[("lx", n)]
        for (dsts, tbl, tlo, prev, key) in (
                (xq, qsc, qlo, ("qp", n - 2, 7), "xqv"),
                (xk, ksc, klo, ("kp", n - 2, 7), "xkv")):
            dst = dsts[p]
            t3h = bass.AP(tbl.tensor, 0, [[CH, 128], [0, 4], [1, CH]])
            t3l = bass.AP(tlo.tensor, 0, [[CH, 128], [0, 4], [1, CH]])
            for h in range(2):
                w = [lx, ld_qsc, ld_ksc, ld_qlo, ld_klo]
                if n > 1 and h == 0:
                    w.append(E[prev])
                sl = xbf[p][:, 4 * h:4 * h + 4, :]
                P.op("V", E_tt(xtA, sl, t3h, ALU.mult), waits=w, inc="v")
                P.op("V", E_tt(xtB, sl, t3l, ALU.mult), inc="v")
                ev = P.op("V", E_tt(dst[:, 4 * h:4 * h + 4, :], xtA, xtB, ALU.add),
                          inc="v")
            E[(key, n)] = ev

    tfree = [None, None]
    tpar = [0]
    tseq = [0]

    def t_load(src_ap, extra=()):
        par = tpar[0]
        tpar[0] = 1 - par
        lw = QG.issue(dma(tstg[par], src_ap), waits=[tfree[par]] + list(extra))
        return par, lw

    def t_quant(par, lw, scale_col, h_col, nh_col, dst_ap):
        k = tseq[0]
        tseq[0] += 1
        P.op("V", E_ts(t1b, tstg[par], h_col, None, ALU.is_gt),
             waits=[lw, E.get(("tksg", k - 1))], inc="v")
        P.op("V", E_ts(tstg[par], tstg[par], nh_col, None, ALU.is_lt), inc="v")
        evy = P.op("V", E_tt(tstg[par], t1b, tstg[par], ALU.subtract), inc="v")
        E[("tksg", k)] = evy
        kp = k % 2
        evw = P.op("V", E_ts(twb[kp], tstg[par], scale_col, None, ALU.mult),
                   waits=[E.get(("ttr", k - 2))], inc="v")
        tfree[par] = evw
        evt = QS.issue((lambda d_, s_: lambda e: e.dma_start_transpose(d_, s_))(
            tsg[kp], twb[kp]), waits=[evw, E.get(("twr", k - 2))])
        E[("ttr", k)] = evt
        E[("twr", k)] = QY.issue(dma(dst_ap, tsg[kp]), waits=[evt])
        return E[("twr", k)]

    def emit_w1_chunk(c):
        par, lw = t_load(wfc_d[c * 128:(c + 1) * 128, :])
        ap3 = bass.AP(tstg[par].tensor, 0, [[D, 128], [128, 8], [1, 128]])
        P.op("V", E_red(tcols[:, 0:8], ap3, ALU.add, absval=True), waits=[lw], inc="v")
        P.op("V", E_red(tcols[:, 12:13], tcols[:, 0:8], ALU.add), inc="v")
        P.op("V", E_ts(tcols[:, 12:13], tcols[:, 12:13], 1.0 / D, 1e-5,
                       ALU.mult, ALU.max), inc="v")
        P.op("V", E_ts(tcols[:, 13:14], tcols[:, 12:13], 0.5, None, ALU.mult), inc="v")
        P.op("V", E_ts(tcols[:, 14:15], tcols[:, 12:13], -0.5, None, ALU.mult), inc="v")
        dst = bass.AP(wt1_scr.tensor, c * 128, [[8 * HID, 128], [HID, 8], [1, 128]])
        t_quant(par, lw, tcols[:, 12:13], tcols[:, 13:14], tcols[:, 14:15], dst)

    def emit_w2_pass1(c2):
        for q in range(4):
            par, lw = t_load(wpj_d[c2 * 128:(c2 + 1) * 128, q * 1024:(q + 1) * 1024])
            ap3 = bass.AP(tstg[par].tensor, 0, [[D, 128], [128, 8], [1, 128]])
            P.op("V", E_red(tcols[:, 0:8], ap3, ALU.add, absval=True),
                 waits=[lw], inc="v")
            ev = P.op("V", E_red(tcols[:, 8 + q:9 + q], tcols[:, 0:8], ALU.add),
                      inc="v")
            tfree[par] = ev
        P.op("V", E_red(tcols[:, 16 + c2:17 + c2], tcols[:, 8:12], ALU.add), inc="v")
        P.op("V", E_ts(tcols[:, 16 + c2:17 + c2], tcols[:, 16 + c2:17 + c2],
                       1.0 / HID, 1e-5, ALU.mult, ALU.max), inc="v")
        P.op("V", E_ts(tcols[:, 24 + c2:25 + c2], tcols[:, 16 + c2:17 + c2],
                       0.5, None, ALU.mult), inc="v")

    def emit_w2_pass2(c2):
        for q in range(4):
            par, lw = t_load(wpj_d[c2 * 128:(c2 + 1) * 128, q * 1024:(q + 1) * 1024])
            dst = bass.AP(wt2_scr.tensor, (q * 8) * D + c2 * 128,
                          [[32 * D, 128], [D, 8], [1, 128]])
            P.op("V", E_ts(tcols[:, 15:16], tcols[:, 24 + c2:25 + c2],
                           -1.0, None, ALU.mult), inc="v")
            t_quant(par, lw, tcols[:, 16 + c2:17 + c2], tcols[:, 24 + c2:25 + c2],
                    tcols[:, 15:16], dst)

    emit_xscales(0)
    for n in range(NCH):
        p = n % 2
        lx = E[("lx", n)]

        for co in range(8):
            bank = co % 2
            w = [E[("xqv", n)], ld_wq, ev_bankfree[bank]]
            for ci in range(8):
                ev = P.op("T", E_mm(pb[bank], wq[:, ci, co * 128:(co + 1) * 128],
                                    xq[p][:, ci, :], ci == 0, ci == 7),
                          waits=(w if ci == 0 else ()), inc=("t" if ci == 7 else None))
            E[("qp", n, co)] = ev
            eve = P.op("S", E_cp(qt[:, co, :], pb[bank]), waits=[ev], inc="s")
            E[("qe", n, co)] = eve
            ev_bankfree[bank] = eve
        for co in range(8):
            bank = 2 + co % 2
            w = [E[("xkv", n)], ld_wk, ev_bankfree[bank]]
            for ci in range(8):
                ev = P.op("T", E_mm(pb[bank], wk[:, ci, co * 128:(co + 1) * 128],
                                    xk[p][:, ci, :], ci == 0, ci == 7),
                          waits=(w if ci == 0 else ()), inc=("t" if ci == 7 else None))
            E[("kp", n, co)] = ev
            eve = P.op("S", E_cp(kf[:, co, :], pb[bank]),
                       waits=[ev, (E[("kte", n - 1)] if (n > 0 and co == 0) else None)],
                       inc="s")
            E[("ke", n, co)] = eve
            ev_bankfree[bank] = eve
        for sc in range(4):
            bank = sc % 2
            w = [lx, ld_wv, ev_bankfree[bank]]
            for ci in range(8):
                ev = P.op("T", E_mm(pb[bank], xbf[p][:, ci, sc * 128:(sc + 1) * 128],
                                    wv[:, ci, :], ci == 0, ci == 7),
                          waits=(w if ci == 0 else ()), inc=("t" if ci == 7 else None))
            E[("vp", n, sc)] = ev
            eve = P.op("S", E_cp(vt[:, sc, :], pb[bank]), waits=[ev], inc="s")
            E[("ve", n, sc)] = eve
            ev_bankfree[bank] = eve
        dst = bass.AP(kt.tensor, 0, [[4096, 128], [128, 32], [1, 128]])
        srcf = bass.AP(kf.tensor, 0, [[4096, 128], [1, 4096]])
        w = [E[("ke", n, 7)], (E[("stp", n - 1, 7)] if n > 0 else None)]
        ev = QS.issue((lambda d_, s_: lambda e: e.dma_start_transpose(d_, s_))(
            dst, srcf), waits=w)
        E[("kte", n)] = ev
        if n + 1 < NCH:
            emit_xscales(n + 1)
        for sc in range(4):
            bank = 4 + sc % 2
            w = [E[("qe", n, 7)], E[("ke", n, 7)], ev_bankfree[bank]]
            for ci in range(8):
                ev = P.op("T", E_mm(pb[bank], kf[:, ci, sc * 128:(sc + 1) * 128],
                                    qt[:, ci, :], ci == 0, ci == 7),
                          waits=(w if ci == 0 else ()), inc=("t" if ci == 7 else None))
            E[("sp", n, sc)] = ev
            eve = P.op("V", E_tt(am[:, sc, :], pb[bank], mask[:, sc, :], ALU.mult),
                       waits=[ev, ld_msk], inc="v")
            E[("ame", n, sc)] = eve
            ev_bankfree[bank] = eve
        for dc in range(4):
            bank = 6 + dc % 2
            w = [ev_bankfree[bank], E[("ame", n, 3)], E[("ve", n, 3)], E[("qe", n, 7)]]
            if n > 0:
                w.append(E[("se", n - 1, 7)])
            nmm = (8 if n > 0 else 0) + 4
            i = 0
            if n > 0:
                for kc in range(8):
                    ev = P.op("T", E_mm(pb[bank], sbf[(n - 1) % 2][:, kc, dc * 128:(dc + 1) * 128],
                                        qt[:, kc, :], kc == 0, False),
                              waits=(w if i == 0 else ()), inc=None)
                    i += 1
            for sc in range(4):
                ev = P.op("T", E_mm(pb[bank], vt[:, sc, dc * 128:(dc + 1) * 128],
                                    am[:, sc, :], (n == 0 and sc == 0), sc == 3),
                          waits=(w if i == 0 else ()), inc=("t" if sc == 3 else None))
                i += 1
            E[("op", n, dc)] = ev
            wv2 = [ev]
            if n >= 2:
                wv2.append(E[("od", n - 2)])
            eve = P.op("S", E_cp(oat[p][:, dc, :], pb[bank]), waits=wv2, inc="s")
            E[("oe", n, dc)] = eve
            ev_bankfree[bank] = eve
        dst = bass.AP(scr.tensor, n * CH, [[T, 128], [128 * T, 4], [1, CH]])
        E[("od", n)] = QY.issue(dma(dst, oat[p]), waits=[E[("oe", n, 3)]])
        for kc in range(8):
            bank = kc % 4
            w = [E[("kte", n)], E[("ve", n, 3)], ev_bankfree[bank], ld_igc]
            if n > 0:
                w.append(E[("se", n - 1, kc)])
            for sc in range(4):
                ev = P.op("T", E_mm(pb[bank], kt[:, 4 * kc + sc, :],
                                    vt[:, sc, :], sc == 0, (sc == 3 and n == 0)),
                          waits=(w if sc == 0 else ()),
                          inc=("t" if (sc == 3 and n == 0) else None))
            if n > 0:
                ev = P.op("T", E_mm(pb[bank], igc, sbf[(n - 1) % 2][:, kc, :],
                                    False, True), inc="t")
            E[("stp", n, kc)] = ev
            eve = P.op("S", E_act(sbf[n % 2][:, kc, :], pb[bank], AF.Copy,
                                  scale=G_C),
                       waits=[ev], inc="s")
            E[("se", n, kc)] = eve
            ev_bankfree[bank] = eve
        if n + 2 < NCH:
            w = [E[("xkv", n)], E[("kp", n, 7)], E[("vp", n, 3)]]
            E[("lx", n + 2)] = QG.issue(dma(*ld_x(n + 2)), waits=w)
        if 8 <= n < 16:
            emit_w2_pass2(n - 8)
        if n < 6:
            for c in range(4 * n, 4 * n + 4):
                emit_w1_chunk(c)
        elif n < 8:
            emit_w1_chunk(24 + 2 * (n - 6))
            emit_w1_chunk(25 + 2 * (n - 6))
        elif n < 12:
            emit_w1_chunk(28 + (n - 8))
        if 6 <= n < 14:
            emit_w2_pass1(n - 6)
        if n == 11:
            E[("w1wr_done",)] = E[("twr", tseq[0] - 1)]

    BAR = [P.now("t"), P.now("v"), P.now("s")] + QY.done() + QG.done()

    if dev_phase1_only:
        P.op("Y", lambda e: e.nop(), waits=list(BAR))
        return nc, P

    w1_ready = QG.issue(dma(w1t, wt1_scr),
                        waits=[E[("kp", NCH - 1, 7)], E[("vp", NCH - 1, 3)],
                               E[("xkv", NCH - 1)], E[("ame", NCH - 1, 3)],
                               E[("w1wr_done",)]])
    ld_id = QY.issue(dma(idn, id_d), waits=BAR)
    ld_vt2 = QG.issue(dma(vt2, bass.AP(vt_d.tensor, 0, [[16, 128], [128 * 16, 8], [1, 16]])),
                      waits=BAR)
    ld_ut = QG.issue(dma(ut[0:16, :], ut_d), waits=BAR)
    ld_utB = QG.issue(dma(ut[32:48, :], ut_d), waits=BAR)
    ld_fbt = QG.issue(dma(fbt, fbt_d), waits=BAR)
    ld_gate = QY.issue(dma(gate, gate_d), waits=BAR)

    w2_ready = QG.issue(dma(w2t, wt2_scr), waits=list(BAR))
    WTT = w2_ready

    inv_d = 1.0 / D

    def ln(buf_ap, dump_ap, c_m, c_v, c_r, pre_waits):
        P.op("V", E_red(c_m, buf_ap, ALU.add, negate=True), waits=pre_waits, inc="v")
        P.op("V", E_ts(c_m, c_m, inv_d, None, ALU.mult), inc="v")
        P.op("V", E_ts(buf_ap, buf_ap, c_m, None, ALU.add), inc="v")
        P.op("V", E_tt(dump_ap, buf_ap, buf_ap, ALU.mult), inc="v")
        P.op("V", E_red(c_v, dump_ap, ALU.add), inc="v")
        P.op("V", E_ts(c_v, c_v, inv_d, 1e-5, ALU.mult, ALU.add), inc="v")
        evr = P.op("V", E_recip(c_r, c_v), inc="v")
        evs = P.op("S", E_act(c_r, c_r, AF.Sqrt), waits=[evr], inc="s")
        return P.op("V", E_ts(buf_ap, buf_ap, c_r, None, ALU.mult),
                    waits=[evs], inc="v")

    def emit_front_dma(hs):
        j, half = divmod(hs, 2)
        q2 = hs % 2
        wA = [E[("od", 2 * j + 1)], (None if hs < 2 else E[("h1fe", hs - 2)])]
        if hs < 2:
            wA += BAR
        src = bass.AP(scr.tensor, half * 2 * 128 * T + j * D,
                      [[T, 128], [128 * T, 2], [1, D]])
        E[("al", hs)] = QY.issue(dma(att[q2], src), waits=wA)
        for tb in range(2):
            dc = 2 * half + tb
            wX = [(None if hs < 2 else E[("ln2d", hs - 2, tb)])]
            if hs < 2:
                wX += BAR
            src = bass.AP(xres.tensor, (dc * 1024 + j) * D, [[8 * D, 128], [1, D]])
            E[("xl", hs, tb)] = QY.issue(dma(h1[q2][:, tb, :], src), waits=wX)

    def front_steps(hs):
        q2 = hs % 2
        st = {}
        steps = []
        for tb in range(2):
            buf = h1[q2][:, tb, :]
            dump = outf[q2][:, tb, :]
            atb = att[q2][:, tb, :]
            c_m, c_v, c_r = cols[:, 2:3], cols[:, 3:4], cols[:, 4:5]

            def s_pre(tb=tb, buf=buf, atb=atb, c_m=c_m):
                P.op("V", E_tt(buf, buf, atb, ALU.add),
                     waits=[E[("al", hs)], E[("xl", hs, tb)]], inc="v")
                P.op("V", E_red(c_m, buf, ALU.add, negate=True), inc="v")

            def s_mid1(buf=buf, c_m=c_m):
                P.op("V", E_ts(c_m, c_m, inv_d, None, ALU.mult), inc="v")
                P.op("V", E_ts(buf, buf, c_m, None, ALU.add), inc="v")

            def s_mid2(tb=tb, buf=buf, dump=dump, c_v=c_v):
                wdump = [] if hs < 2 else [E[("oD", hs - 2, tb)]]
                P.op("V", E_tt(dump, buf, buf, ALU.mult), waits=wdump, inc="v")
                P.op("V", E_red(c_v, dump, ALU.add), inc="v")

            def s_mid3(tb=tb, c_v=c_v, c_r=c_r):
                P.op("V", E_ts(c_v, c_v, inv_d, 1e-5, ALU.mult, ALU.add), inc="v")
                evr = P.op("V", E_recip(c_r, c_v), inc="v")
                st[tb] = P.op("S", E_act(c_r, c_r, AF.Sqrt), waits=[evr], inc="s")

            def s_fin(tb=tb, buf=buf, atb=atb, c_r=c_r):
                P.op("V", E_ts(buf, buf, c_r, None, ALU.mult),
                     waits=[st[tb]], inc="v")
                evc = P.op("V", E_tt2cp(atb, buf),
                           waits=(list(BAR) if hs < 2 else []), inc="v")
                dst = bass.AP(h1fp[hs % 2].tensor, tb * 128,
                              [[8 * 256, 128], [256, 8], [1, 128]])
                eve = QS.issue((lambda d_, s_: lambda e: e.dma_start_transpose(d_, s_))(
                    dst, atb),
                    waits=[evc, (None if hs < 2 else E[("mm1", hs - 2, 31)])])
                if tb == 1:
                    E[("h1fe", hs)] = eve

            steps += [s_pre, s_mid1, s_mid2, s_mid3, s_fin]
        return steps

    def emit_lr(hs):
        w = [E[("h1fe", hs)], ld_vt2, ev_bankfree[4]]
        for ci in range(8):
            ev = P.op("T", E_mm(pb[4][:16, 0:256], vt2[:, ci, :],
                                h1fp[hs % 2][:, ci, :], ci == 0, ci == 7),
                      waits=(w if ci == 0 else ()), inc=("t" if ci == 7 else None))
        ev1e = P.op("S", E_cp(lr1, pb[4][:16, 0:256]), waits=[ev], inc="s")
        ev_bankfree[4] = ev1e
        ev2m = P.op("T", E_mm(pb[5][0:16, 0:256], fbt, lr1, True, True),
                    waits=[ev1e, ld_fbt, ev_bankfree[5]], inc="t")
        ev2mB = P.op("T", E_mm(pb[5][32:48, 0:256], fbt, lr1, True, True), inc="t")
        ev2e = P.op("S", E_act(lr2[0:16, :], pb[5][0:16, 0:256], AF.Copy,
                               scale=gate[:16, :]),
                    waits=[ev2m, ld_gate], inc="s")
        ev2eB = P.op("S", E_act(lr2[32:48, :], pb[5][32:48, 0:256], AF.Copy,
                                scale=gate[32:48, :]),
                     waits=[ev2mB], inc="s")
        ev_bankfree[5] = ev2eB
        E[("lr2e", hs)] = (ev2e, ev2eB)

    emit_front_dma(0)
    for s in front_steps(0):
        s()
    emit_lr(0)
    FS_SCHED = {3: 0, 5: 1, 7: 2, 9: 3, 10: 4, 11: 5, 12: 6, 13: 7, 14: 8}
    for hs in range(NHS):
        j, half = divmod(hs, 2)
        q2 = hs % 2
        ev2e, ev2eB = E[("lr2e", hs)]
        fsteps = front_steps(hs + 1) if hs + 1 < NHS else None
        for hp in range(16):
            if fsteps is not None:
                if hp == 0:
                    emit_front_dma(hs + 1)
                elif hp in FS_SCHED:
                    fsteps[FS_SCHED[hp]]()
            hcA, hcB = 2 * hp, 2 * hp + 1
            bkA, bkB = hcA % 6, hcB % 6
            for hc, bank in ((hcA, bkA), (hcB, bkB)):
                w = [E[("h1fe", hs)], ev2e, ev2eB, ev_bankfree[bank], ld_ut, ld_utB,
                     (w1_ready if hs == 0 and hc == 0 else None)]
                for cc in range(8):
                    P.op("T", E_mm(pb[bank][:, 0:256],
                                   w1t[:, cc, hc * 128:(hc + 1) * 128],
                                   h1fp[hs % 2][:, cc, :], cc == 0, False),
                         waits=(w if cc == 0 else ()), inc=None)
            evA = P.op("T", E_mmt(pb[bkA][:, 0:256], ut[0:16, hcA * 128:(hcA + 1) * 128],
                                  lr2[0:16, :], False, True, (0, 0)), inc="t")
            evB = P.op("T", E_mmt(pb[bkB][:, 0:256], ut[32:48, hcB * 128:(hcB + 1) * 128],
                                  lr2[32:48, :], False, True, (32, 0)), inc="t")
            E[("mm1", hs, hcA)] = evA
            E[("mm1", hs, hcB)] = evB
            for hc, bank, ev in ((hcA, bkA, evA), (hcB, bkB, evB)):
                rt_free = (E[("sq", hs, hc - 2)] if hc >= 2 else
                           (E[("sq", hs - 1, 30 + hc)] if hs > 0 else None))
                evr = P.op("S", E_act(rtmp[:, hc % 2, :], pb[bank][:, 0:256], AF.Relu),
                           waits=[ev, rt_free], inc="s")
                evq = P.op("S", E_act(r2[:, hc, :], rtmp[:, hc % 2, :], AF.Square),
                           waits=[evr], inc="s")
                E[("sq", hs, hc)] = evq
                ev_bankfree[bank] = evq
        if fsteps is not None:
            fsteps[9]()
        for tb in range(2):
            for csi, cs in enumerate((0, 1)):
                bank = 6 + csi
                w = [E[("sq", hs, 31)], ev_bankfree[bank],
                     (w2_ready if hs == 0 else None)]
                for hc in range(32):
                    ev = P.op("T", E_mm(pb[bank], r2[:, hc, tb * 128:(tb + 1) * 128],
                                        w2t[:, hc, cs * 512:(cs + 1) * 512],
                                        hc == 0, hc == 31),
                              waits=(w if hc == 0 else ()), inc=("t" if hc == 31 else None))
                E[("mm2", hs, tb, cs)] = ev
                eve = P.op("V", E_tt(outf[q2][:, tb, cs * 512:(cs + 1) * 512], pb[bank],
                                     h1[q2][:, tb, cs * 512:(cs + 1) * 512], ALU.add),
                           waits=[ev, (None if hs < 2 else E[("oD", hs - 2, tb)])], inc="v")
                ev_bankfree[bank] = eve
            if tb == 0 and hs + 1 < NHS:
                emit_lr(hs + 1)
            evl = ln(outf[q2][:, tb, :], h1[q2][:, tb, :],
                     cols[:, 5:6], cols[:, 6:7], cols[:, 7:8], [eve])
            E[("ln2d", hs, tb)] = evl
            dc = 2 * half + tb
            dst = bass.AP(out_d.tensor, (dc * 1024 + j) * D, [[8 * D, 128], [1, D]])
            E[("oD", hs, tb)] = QY.issue(dma(dst, outf[q2][:, tb, :]), waits=[evl])

    P.op("Y", lambda e: e.nop(), waits=QY.done() + QG.done())
    return nc, P


def _finish(nc, P):
    sems = {}
    import contextlib
    stack = contextlib.ExitStack()
    names = ["t", "v", "s", "ts0"] + [f"g{i}" for i in range(4)] + [f"y{i}" for i in range(6)]
    for name in names:
        sems[name] = stack.enter_context(nc.semaphore(name))
    blk = stack.enter_context(nc.Block())
    blk.tensor(lambda e: P.emit("T", e, sems))
    blk.vector(lambda e: P.emit("V", e, sems))
    blk.scalar(lambda e: P.emit("S", e, sems))
    blk.sync(lambda e: P.emit("Y", e, sems))
    blk.gpsimd(lambda e: P.emit("G", e, sems))
    stack.close()
    return nc


def host_inputs(X, W_Q, W_K, W_V, w_fc, w_proj, u, v, fast_b, gate):
    f32 = np.float32
    import ml_dtypes
    i = np.arange(CH, dtype=np.float64)
    g = np.float64(GAMMA)
    q64, k64 = g ** i, g ** (-i)
    qrow, krow = q64.astype(f32), k64.astype(f32)
    qlo = (q64 - qrow.astype(np.float64)).astype(f32)
    klo = (k64 - krow.astype(np.float64)).astype(f32)
    rep = lambda r: np.ascontiguousarray(np.broadcast_to(r, (128, CH)))
    p = np.arange(128)[:, None, None]
    sc = np.arange(4)[None, :, None]
    q = np.arange(CH)[None, None, :]
    mask = (q >= sc * 128 + p).astype(ml_dtypes.bfloat16)
    common = {
        "wq": np.ascontiguousarray(W_Q, f32),
        "wk": np.ascontiguousarray(W_K, f32),
        "wfc": np.ascontiguousarray(w_fc, f32),
        "wpj": np.ascontiguousarray(w_proj, f32),
        "vt_in": np.ascontiguousarray(v.T, f32),
        "ut_in": np.ascontiguousarray(u.T, f32),
        "fbt_in": np.ascontiguousarray(fast_b.T, f32),
        "gate_in": np.ascontiguousarray(np.broadcast_to(
            np.asarray(gate, f32).reshape(1, 1), (128, 1))),
        "mask_in": np.ascontiguousarray(mask),
        "qscale": rep(qrow), "kscale": rep(krow),
        "qscale_lo": rep(qlo), "kscale_lo": rep(klo),
        "ident": np.eye(128, dtype=f32),
        "igc_in": np.eye(128, dtype=f32),
    }
    in_maps = []
    for k in range(8):
        b, h = k // 2, k % 2
        m = dict(common)
        m["xfm"] = np.ascontiguousarray(X[b].T.astype(f32, copy=False))
        m["xres"] = np.ascontiguousarray(X[b, h * 4096:(h + 1) * 4096].astype(f32, copy=False))
        m["wv"] = np.ascontiguousarray(W_V[:, h * DH:(h + 1) * DH], f32)
        in_maps.append(m)
    return in_maps


def kernel(L0_attn_X, L0_attn_att_mask, L0_attn_S_n, L0_attn_W_Q, L0_attn_W_K,
           L0_attn_W_V, L0_ffn_fast_b, L0_ffn_w_fc, L0_ffn_w_proj, L0_ffn_u,
           L0_ffn_v, L0_ffn_gate):
    X = np.asarray(L0_attn_X)
    in_maps = host_inputs(X, np.asarray(L0_attn_W_Q), np.asarray(L0_attn_W_K),
                          np.asarray(L0_attn_W_V), np.asarray(L0_ffn_w_fc),
                          np.asarray(L0_ffn_w_proj), np.asarray(L0_ffn_u),
                          np.asarray(L0_ffn_v), np.asarray(L0_ffn_fast_b),
                          np.asarray(L0_ffn_gate))
    nc, P = build_nc()
    nc = _finish(nc, P)
    res = run_bass_kernel_spmd(nc, in_maps, core_ids=list(range(8)))
    y = np.empty((B, T, D), dtype=np.float32)
    for k in range(8):
        b, h = k // 2, k % 2
        y[b, h * 4096:(h + 1) * 4096] = res.results[k]["out"]
    return y


# version 32 (final)
# speedup vs baseline: 1.0640x; 1.0640x over previous
import numpy as np
import concourse.bass as bass
import concourse.mybir as mybir
from concourse.bass_utils import run_bass_kernel_spmd

F32 = mybir.dt.float32
BF16 = mybir.dt.bfloat16
AF = mybir.ActivationFunctionType
ALU = mybir.AluOpType
AX = mybir.AxisListType.X

GAMMA = 0.99
B, T, D = 4, 8192, 1024
CH = 512
NCH = T // CH
DH = 512
HID = 4096
NHS = 16
_gc64 = np.float64(GAMMA) ** CH
G_C = float(np.float32(_gc64))
G_C_LO = float(np.float32(_gc64 - np.float64(G_C)))


class Prog:
    ENGS = ("T", "V", "S", "Y", "G")

    def __init__(self):
        self.ops = {e: [] for e in self.ENGS}
        self.cnt = {}
        self.rw = {}
        self.owner = {"t": "T", "v": "V", "s": "S"}

    SELF_SEM = {"V": "v", "S": "s"}
    PAGE = 2048

    def _regions(self, ap):
        t = getattr(ap, "tensor", None)
        if t is None:
            return ()
        rng = getattr(t, "manual_sbuf_range", None)
        if rng:
            return [("S", p) for p in range(rng[0] // self.PAGE,
                                            (rng[1] - 1) // self.PAGE + 1)]
        return [("N", t.name)]

    def op(self, eng, emit, waits=(), inc=None, amt=1):
        w = []
        for x in waits:
            if x is None:
                continue
            sem, val = x
            if self.owner.get(sem) == eng:
                continue
            w.append((sem, val))
        own = self.SELF_SEM.get(eng)
        if own is not None:
            hist = self.rw.setdefault(eng, {})
            k = self.cnt.get(own, 0)
            if hasattr(emit, "w"):
                need = 0
                rd = [r for a in emit.r for r in self._regions(a)]
                wr = [r for a in emit.w for r in self._regions(a)]
                for r in rd:
                    e2 = hist.get(r)
                    if e2:
                        need = max(need, e2[0])
                for r in wr:
                    e2 = hist.get(r)
                    if e2:
                        need = max(need, e2[0], e2[1])
                if need > 0:
                    w.append((own, need))
                idx = k + 1
                for r in rd:
                    e2 = hist.setdefault(r, [0, 0])
                    e2[1] = max(e2[1], idx)
                for r in wr:
                    e2 = hist.setdefault(r, [0, 0])
                    e2[0] = idx
                    e2[1] = max(e2[1], idx)
            else:
                if k > 0:
                    w.append((own, k))
        ev = None
        if inc is not None:
            self.cnt[inc] = self.cnt.get(inc, 0) + amt
            ev = (inc, self.cnt[inc])
            self.ops[eng].append((w, emit, (inc, amt)))
        else:
            self.ops[eng].append((w, emit, None))
        return ev

    def now(self, sem):
        return (sem, self.cnt.get(sem, 0))

    def emit(self, eng_key, engine, sems):
        floors = {}
        for waits, emit, inc in self.ops[eng_key]:
            tgt = {}
            for s, v in waits:
                if v > floors.get(s, 0):
                    tgt[s] = max(tgt.get(s, 0), v)
            for s in sorted(tgt):
                engine.wait_ge(sems[s], tgt[s])
                floors[s] = tgt[s]
            inst = emit(engine)
            if inc is not None:
                s, amt = inc
                inst.then_inc(sems[s], amt)


class DQ:

    def __init__(self, P, eng, base, R=4):
        self.P, self.eng = P, eng
        self.names = [f"{base}{i}" for i in range(R)]
        self.i = 0

    def issue(self, emit, waits=()):
        s = self.names[self.i]
        self.i = (self.i + 1) % len(self.names)
        prior = self.P.cnt.get(s, 0)
        w = list(waits)
        if prior:
            w.append((s, prior))
        return self.P.op(self.eng, emit, waits=w, inc=s, amt=16)

    def done(self):
        return [self.P.now(s) for s in self.names]



def _tag(f, writes, reads, scalars=()):
    f.w = [x for x in writes if x is not None]
    f.r = [x for x in reads if x is not None] + \
          [x for x in scalars if hasattr(x, "tensor")]
    return f

def E_mm(out, l, r, st, sp):
    return _tag(lambda e: e.matmul(out, l, r, start=st, stop=sp), [out], [l, r])

def E_tt(out, a, b2, op):
    return _tag(lambda e: e.tensor_tensor(out, a, b2, op), [out], [a, b2])

def E_mmt(out, l, r, st, sp, tp):
    return _tag(lambda e: e.matmul(out, l, r, start=st, stop=sp, tile_position=tp),
                [out], [l, r])

def E_ts(out, a, s1, s2, op0, op1=None):
    if op1 is None:
        return _tag(lambda e: e.tensor_scalar(out, a, s1, None, op0), [out], [a], [s1])
    return _tag(lambda e: e.tensor_scalar(out, a, s1, s2, op0, op1), [out], [a], [s1, s2])

def E_stt(out, a, sc, b2, op0, op1):
    return _tag(lambda e: e.scalar_tensor_tensor(out, a, sc, b2, op0, op1),
                [out], [a, b2], [sc])

def E_cp(out, a):
    return _tag(lambda e: e.copy(out, a), [out], [a])

def E_act(out, a, f, **kw):
    return _tag(lambda e: e.activation(out, a, f, **kw), [out], [a],
                [kw.get("scale"), kw.get("bias")])

def E_tr(out, a, ident):
    return _tag(lambda e: e.transpose(out, a, ident), [out], [a, ident])

def E_red(out, a, op, negate=False, absval=False):
    return _tag(lambda e: e.tensor_reduce(out, a, AX, op, apply_absolute_value=absval,
                                          negate=negate), [out], [a])

def E_ttr(out, a, b2, scale, init, op0, op1, acc):
    return _tag(lambda e: e.tensor_tensor_reduce(out, a, b2, scale, init, op0, op1, acc),
                [out, acc], [a, b2])

def E_memset(ap, val):
    return _tag(lambda e: e.memset(ap, val), [ap], [])

def E_tt2cp(out, a):
    return _tag(lambda e: e.tensor_copy(out, a), [out], [a])

def E_recip(out, a):
    return _tag(lambda e: e.reciprocal(out, a), [out], [a])


def build_nc(dev_phase1_only=False):
    nc = bass.Bass("TRN2", target_bir_lowering=False)

    xfm = nc.dram_tensor("xfm", [D, T], F32, kind="ExternalInput").ap()
    xres = nc.dram_tensor("xres", [4096, D], F32, kind="ExternalInput").ap()
    wq_d = nc.dram_tensor("wq", [D, D], F32, kind="ExternalInput").ap()
    wk_d = nc.dram_tensor("wk", [D, D], F32, kind="ExternalInput").ap()
    wv_d = nc.dram_tensor("wv", [D, DH], F32, kind="ExternalInput").ap()
    wfc_d = nc.dram_tensor("wfc", [HID, D], F32, kind="ExternalInput").ap()
    wpj_d = nc.dram_tensor("wpj", [D, HID], F32, kind="ExternalInput").ap()
    vt_d = nc.dram_tensor("vt_in", [D, 16], F32, kind="ExternalInput").ap()
    ut_d = nc.dram_tensor("ut_in", [16, HID], F32, kind="ExternalInput").ap()
    fbt_d = nc.dram_tensor("fbt_in", [16, 16], F32, kind="ExternalInput").ap()
    gate_d = nc.dram_tensor("gate_in", [128, 1], F32, kind="ExternalInput").ap()
    mask_d = nc.dram_tensor("mask_in", [128, 4, CH], BF16, kind="ExternalInput").ap()
    qsc_d = nc.dram_tensor("qscale", [128, CH], F32, kind="ExternalInput").ap()
    ksc_d = nc.dram_tensor("kscale", [128, CH], F32, kind="ExternalInput").ap()
    qlo_d = nc.dram_tensor("qscale_lo", [128, CH], F32, kind="ExternalInput").ap()
    klo_d = nc.dram_tensor("kscale_lo", [128, CH], F32, kind="ExternalInput").ap()
    id_d = nc.dram_tensor("ident", [128, 128], F32, kind="ExternalInput").ap()
    igc_d = nc.dram_tensor("igc_in", [128, 128], F32, kind="ExternalInput").ap()
    out_d = nc.dram_tensor("out", [4096, D], F32, kind="ExternalOutput").ap()
    scr_kind = "ExternalOutput" if dev_phase1_only else "Internal"
    scr = nc.dram_tensor("oat_scr", [DH, T], BF16, kind=scr_kind).ap()
    wt1_scr = nc.dram_tensor("wt1_scr", [128, 8, HID], BF16, kind="Internal").ap()
    wt2_scr = nc.dram_tensor("wt2_scr", [128, 32, D], BF16, kind="Internal").ap()

    SBB = 16544

    def sb(name, shape, dtype, off):
        return nc.alloc_sbuf_tensor_at(name, shape, dtype, offset=SBB + off).ap()

    wq = sb("wq_sb", [128, 8, D], BF16, 0)
    wk = sb("wk_sb", [128, 8, D], BF16, 16384)
    wv = sb("wv_sb", [128, 8, DH], BF16, 32768)
    mask = sb("mask_sb", [128, 4, CH], BF16, 40960)
    qsc = sb("qsc_sb", [128, CH], F32, 45056)
    ksc = sb("ksc_sb", [128, CH], F32, 47104)
    qlo = sb("qlo_sb", [128, CH], F32, 49152)
    xbf = [sb(f"xbf{i}", [128, 8, CH], BF16, 51200 + i * 8192) for i in range(2)]
    xq = [sb(f"xq{i}_sb", [128, 8, CH], BF16, [67584, 172032][i]) for i in range(2)]
    xk = [sb(f"xk{i}_sb", [128, 8, CH], BF16, [75776, 180224][i]) for i in range(2)]
    qt = sb("qt_sb", [128, 8, CH], BF16, 92160)
    kf = sb("kf_sb", [128, 8, CH], BF16, 100352)
    kt = sb("kt_sb", [128, 32, 128], BF16, 108544)
    vt = sb("vt_sb", [128, 4, DH], BF16, 116736)
    am = sb("am_sb", [128, 4, CH], BF16, 120832)
    sbf = [sb(f"sbf{i}", [128, 8, DH], BF16, 124928 + i * 8192) for i in range(2)]
    oat = [sb(f"oat{i}", [128, 4, CH], BF16, 141312 + i * 4096) for i in range(2)]
    klo = sb("klo_sb", [128, CH], F32, 149504)
    xtA = sb("xtA_sb", [128, 4, CH], F32, 155648)
    xtB = sb("xtB_sb", [128, 4, CH], F32, 163840)
    igc = sb("igc_sb", [128, 128], BF16, 188416)
    tstg = [sb(f"tstg{i}", [128, D], F32, 188672 + i * 4096) for i in range(2)]
    t1b = sb("t1b_sb", [128, D], F32, 196864)
    twb = [sb(f"twb{i}_sb", [128, D], BF16, [200960, 205184][i]) for i in range(2)]
    tsg = [sb(f"tsg{i}_sb", [128, 8, 128], BF16, [203008, 207232][i]) for i in range(2)]
    tcols = sb("tcols_sb", [128, 32], F32, 205056)
    w1t = sb("w1t_sb", [128, 8, HID], BF16, 0)
    w2t = sb("w2t_sb", [128, 32, D], BF16, 65536)
    idn = sb("id_sb", [128, 128], F32, 131072)
    vt2 = sb("vt2_sb", [128, 8, 16], BF16, 131584)
    ut = sb("ut_sb", [48, HID], BF16, 131840)
    fbt = sb("fbt_sb", [16, 16], BF16, 140032)
    gate = sb("gate_sb", [128, 1], F32, 140096)
    lr1 = sb("lr1_sb", [16, 256], BF16, 140128)
    lr2 = sb("lr2_sb", [48, 256], BF16, 140672)
    cols = sb("cols_sb", [128, 10], F32, 141216)

    att = [sb(f"att{i}", [128, 2, D], BF16, 180544 + i * 4096) for i in range(2)]
    h1 = [sb(f"h1_{i}", [128, 2, D], F32, 188736 + i * 8192) for i in range(2)]
    h1fp = [sb("h1f0_sb", [128, 8, 256], BF16, 141632),
            sb("h1f1_sb", [128, 8, 256], BF16, 205184)]
    r2 = sb("r2_sb", [128, 32, 256], BF16, 145728)
    outf = [sb(f"outf{i}", [128, 2, D], F32, 162112 + i * 8192) for i in range(2)]
    rtmp = sb("rtmp_sb", [128, 2, 256], F32, 178496)


    pb = [nc.alloc_psum_tensor(f"pb{i}", [128, 512], F32).ap() for i in range(8)]

    P = Prog()
    QG = DQ(P, "G", "g", 4)
    QY = DQ(P, "Y", "y", 6)
    QS = DQ(P, "Y", "ts", 1)
    ev_bankfree = [None] * 8
    E = {}

    def dma(dst, src):
        return lambda e: e.dma_start(out=dst, in_=src)

    ld_wq = QG.issue(dma(wq, bass.AP(wq_d.tensor, 0, [[D, 128], [128 * D, 8], [1, D]])))
    ld_wk = QG.issue(dma(wk, bass.AP(wk_d.tensor, 0, [[D, 128], [128 * D, 8], [1, D]])))
    ld_wv = QG.issue(dma(wv, bass.AP(wv_d.tensor, 0, [[DH, 128], [128 * DH, 8], [1, DH]])))
    ld_igc = QG.issue(dma(igc, igc_d))
    ld_msk = QY.issue(dma(mask, mask_d))
    ld_qsc = QY.issue(dma(qsc, qsc_d))
    ld_ksc = QY.issue(dma(ksc, ksc_d))
    ld_qlo = QY.issue(dma(qlo, qlo_d))
    ld_klo = QY.issue(dma(klo, klo_d))

    def ld_x(n):
        dst = xbf[n % 2]
        src = bass.AP(xfm.tensor, n * CH, [[T, 128], [128 * T, 8], [1, CH]])
        return dst, src

    E[("lx", 0)] = QG.issue(dma(*ld_x(0)))
    E[("lx", 1)] = QG.issue(dma(*ld_x(1)))

    def emit_xscales(n):
        p = n % 2
        lx = E[("lx", n)]
        for (dsts, tbl, tlo, prev, key) in (
                (xq, qsc, qlo, ("qp", n - 2, 7), "xqv"),
                (xk, ksc, klo, ("kp", n - 2, 7), "xkv")):
            dst = dsts[p]
            t3h = bass.AP(tbl.tensor, 0, [[CH, 128], [0, 4], [1, CH]])
            t3l = bass.AP(tlo.tensor, 0, [[CH, 128], [0, 4], [1, CH]])
            for h in range(2):
                w = [lx, ld_qsc, ld_ksc, ld_qlo, ld_klo]
                if n > 1 and h == 0:
                    w.append(E[prev])
                sl = xbf[p][:, 4 * h:4 * h + 4, :]
                P.op("V", E_tt(xtA, sl, t3h, ALU.mult), waits=w, inc="v")
                P.op("V", E_tt(xtB, sl, t3l, ALU.mult), inc="v")
                ev = P.op("V", E_tt(dst[:, 4 * h:4 * h + 4, :], xtA, xtB, ALU.add),
                          inc="v")
            E[(key, n)] = ev

    tfree = [None, None]
    tpar = [0]
    tseq = [0]

    def t_load(src_ap, extra=()):
        par = tpar[0]
        tpar[0] = 1 - par
        lw = QG.issue(dma(tstg[par], src_ap), waits=[tfree[par]] + list(extra))
        return par, lw

    def t_quant(par, lw, scale_col, h_col, nh_col, dst_ap):
        k = tseq[0]
        tseq[0] += 1
        P.op("V", E_ts(t1b, tstg[par], h_col, None, ALU.is_gt),
             waits=[lw, E.get(("tksg", k - 1))], inc="v")
        P.op("V", E_ts(tstg[par], tstg[par], nh_col, None, ALU.is_lt), inc="v")
        evy = P.op("V", E_tt(tstg[par], t1b, tstg[par], ALU.subtract), inc="v")
        E[("tksg", k)] = evy
        kp = k % 2
        evw = P.op("V", E_ts(twb[kp], tstg[par], scale_col, None, ALU.mult),
                   waits=[E.get(("ttr", k - 2))], inc="v")
        tfree[par] = evw
        evt = QS.issue((lambda d_, s_: lambda e: e.dma_start_transpose(d_, s_))(
            tsg[kp], twb[kp]), waits=[evw, E.get(("twr", k - 2))])
        E[("ttr", k)] = evt
        E[("twr", k)] = QY.issue(dma(dst_ap, tsg[kp]), waits=[evt])
        return E[("twr", k)]

    def emit_w1_chunk(c):
        par, lw = t_load(wfc_d[c * 128:(c + 1) * 128, :])
        ap3 = bass.AP(tstg[par].tensor, 0, [[D, 128], [128, 8], [1, 128]])
        P.op("V", E_red(tcols[:, 0:8], ap3, ALU.add, absval=True), waits=[lw], inc="v")
        P.op("V", E_red(tcols[:, 12:13], tcols[:, 0:8], ALU.add), inc="v")
        P.op("V", E_ts(tcols[:, 12:13], tcols[:, 12:13], 1.0 / D, 1e-5,
                       ALU.mult, ALU.max), inc="v")
        P.op("V", E_ts(tcols[:, 13:14], tcols[:, 12:13], 0.5, None, ALU.mult), inc="v")
        P.op("V", E_ts(tcols[:, 14:15], tcols[:, 12:13], -0.5, None, ALU.mult), inc="v")
        dst = bass.AP(wt1_scr.tensor, c * 128, [[8 * HID, 128], [HID, 8], [1, 128]])
        t_quant(par, lw, tcols[:, 12:13], tcols[:, 13:14], tcols[:, 14:15], dst)

    def emit_w2_pass1(c2):
        for q in range(4):
            par, lw = t_load(wpj_d[c2 * 128:(c2 + 1) * 128, q * 1024:(q + 1) * 1024])
            ap3 = bass.AP(tstg[par].tensor, 0, [[D, 128], [128, 8], [1, 128]])
            P.op("V", E_red(tcols[:, 0:8], ap3, ALU.add, absval=True),
                 waits=[lw], inc="v")
            ev = P.op("V", E_red(tcols[:, 8 + q:9 + q], tcols[:, 0:8], ALU.add),
                      inc="v")
            tfree[par] = ev
        P.op("V", E_red(tcols[:, 16 + c2:17 + c2], tcols[:, 8:12], ALU.add), inc="v")
        P.op("V", E_ts(tcols[:, 16 + c2:17 + c2], tcols[:, 16 + c2:17 + c2],
                       1.0 / HID, 1e-5, ALU.mult, ALU.max), inc="v")
        P.op("V", E_ts(tcols[:, 24 + c2:25 + c2], tcols[:, 16 + c2:17 + c2],
                       0.5, None, ALU.mult), inc="v")

    def emit_w2_pass2(c2):
        for q in range(4):
            par, lw = t_load(wpj_d[c2 * 128:(c2 + 1) * 128, q * 1024:(q + 1) * 1024])
            dst = bass.AP(wt2_scr.tensor, (q * 8) * D + c2 * 128,
                          [[32 * D, 128], [D, 8], [1, 128]])
            P.op("V", E_ts(tcols[:, 15:16], tcols[:, 24 + c2:25 + c2],
                           -1.0, None, ALU.mult), inc="v")
            t_quant(par, lw, tcols[:, 16 + c2:17 + c2], tcols[:, 24 + c2:25 + c2],
                    tcols[:, 15:16], dst)

    emit_xscales(0)
    for n in range(NCH):
        p = n % 2
        lx = E[("lx", n)]

        for co in range(8):
            bank = co % 2
            w = [E[("xqv", n)], ld_wq, ev_bankfree[bank]]
            for ci in range(8):
                ev = P.op("T", E_mm(pb[bank], wq[:, ci, co * 128:(co + 1) * 128],
                                    xq[p][:, ci, :], ci == 0, ci == 7),
                          waits=(w if ci == 0 else ()), inc=("t" if ci == 7 else None))
            E[("qp", n, co)] = ev
            eve = P.op("S", E_cp(qt[:, co, :], pb[bank]), waits=[ev], inc="s")
            E[("qe", n, co)] = eve
            ev_bankfree[bank] = eve
        for co in range(8):
            bank = 2 + co % 2
            w = [E[("xkv", n)], ld_wk, ev_bankfree[bank]]
            for ci in range(8):
                ev = P.op("T", E_mm(pb[bank], wk[:, ci, co * 128:(co + 1) * 128],
                                    xk[p][:, ci, :], ci == 0, ci == 7),
                          waits=(w if ci == 0 else ()), inc=("t" if ci == 7 else None))
            E[("kp", n, co)] = ev
            eve = P.op("S", E_cp(kf[:, co, :], pb[bank]),
                       waits=[ev, (E[("kte", n - 1)] if (n > 0 and co == 0) else None)],
                       inc="s")
            E[("ke", n, co)] = eve
            ev_bankfree[bank] = eve
        for sc in range(4):
            bank = sc % 2
            w = [lx, ld_wv, ev_bankfree[bank]]
            for ci in range(8):
                ev = P.op("T", E_mm(pb[bank], xbf[p][:, ci, sc * 128:(sc + 1) * 128],
                                    wv[:, ci, :], ci == 0, ci == 7),
                          waits=(w if ci == 0 else ()), inc=("t" if ci == 7 else None))
            E[("vp", n, sc)] = ev
            eve = P.op("S", E_cp(vt[:, sc, :], pb[bank]), waits=[ev], inc="s")
            E[("ve", n, sc)] = eve
            ev_bankfree[bank] = eve
        dst = bass.AP(kt.tensor, 0, [[4096, 128], [128, 32], [1, 128]])
        srcf = bass.AP(kf.tensor, 0, [[4096, 128], [1, 4096]])
        w = [E[("ke", n, 7)], (E[("stp", n - 1, 7)] if n > 0 else None)]
        ev = QS.issue((lambda d_, s_: lambda e: e.dma_start_transpose(d_, s_))(
            dst, srcf), waits=w)
        E[("kte", n)] = ev
        if n + 1 < NCH:
            emit_xscales(n + 1)
        for sc in range(4):
            bank = 4 + sc % 2
            w = [E[("qe", n, 7)], E[("ke", n, 7)], ev_bankfree[bank]]
            for ci in range(8):
                ev = P.op("T", E_mm(pb[bank], kf[:, ci, sc * 128:(sc + 1) * 128],
                                    qt[:, ci, :], ci == 0, ci == 7),
                          waits=(w if ci == 0 else ()), inc=("t" if ci == 7 else None))
            E[("sp", n, sc)] = ev
            eve = P.op("V", E_tt(am[:, sc, :], pb[bank], mask[:, sc, :], ALU.mult),
                       waits=[ev, ld_msk], inc="v")
            E[("ame", n, sc)] = eve
            ev_bankfree[bank] = eve
        for dc in range(4):
            bank = 6 + dc % 2
            w = [ev_bankfree[bank], E[("ame", n, 3)], E[("ve", n, 3)], E[("qe", n, 7)]]
            if n > 0:
                w.append(E[("se", n - 1, 7)])
            nmm = (8 if n > 0 else 0) + 4
            i = 0
            if n > 0:
                for kc in range(8):
                    ev = P.op("T", E_mm(pb[bank], sbf[(n - 1) % 2][:, kc, dc * 128:(dc + 1) * 128],
                                        qt[:, kc, :], kc == 0, False),
                              waits=(w if i == 0 else ()), inc=None)
                    i += 1
            for sc in range(4):
                ev = P.op("T", E_mm(pb[bank], vt[:, sc, dc * 128:(dc + 1) * 128],
                                    am[:, sc, :], (n == 0 and sc == 0), sc == 3),
                          waits=(w if i == 0 else ()), inc=("t" if sc == 3 else None))
                i += 1
            E[("op", n, dc)] = ev
            wv2 = [ev]
            if n >= 2:
                wv2.append(E[("od", n - 2)])
            eve = P.op("S", E_cp(oat[p][:, dc, :], pb[bank]), waits=wv2, inc="s")
            E[("oe", n, dc)] = eve
            ev_bankfree[bank] = eve
        dst = bass.AP(scr.tensor, n * CH, [[T, 128], [128 * T, 4], [1, CH]])
        E[("od", n)] = QY.issue(dma(dst, oat[p]), waits=[E[("oe", n, 3)]])
        for kc in range(8):
            bank = kc % 4
            w = [E[("kte", n)], E[("ve", n, 3)], ev_bankfree[bank], ld_igc]
            if n > 0:
                w.append(E[("se", n - 1, kc)])
            for sc in range(4):
                ev = P.op("T", E_mm(pb[bank], kt[:, 4 * kc + sc, :],
                                    vt[:, sc, :], sc == 0, (sc == 3 and n == 0)),
                          waits=(w if sc == 0 else ()),
                          inc=("t" if (sc == 3 and n == 0) else None))
            if n > 0:
                ev = P.op("T", E_mm(pb[bank], igc, sbf[(n - 1) % 2][:, kc, :],
                                    False, True), inc="t")
            E[("stp", n, kc)] = ev
            eve = P.op("S", E_act(sbf[n % 2][:, kc, :], pb[bank], AF.Copy,
                                  scale=G_C),
                       waits=[ev], inc="s")
            E[("se", n, kc)] = eve
            ev_bankfree[bank] = eve
        if n + 2 < NCH:
            w = [E[("xkv", n)], E[("kp", n, 7)], E[("vp", n, 3)]]
            E[("lx", n + 2)] = QG.issue(dma(*ld_x(n + 2)), waits=w)
        if 8 <= n < 16:
            emit_w2_pass2(n - 8)
        if n < 6:
            for c in range(4 * n, 4 * n + 4):
                emit_w1_chunk(c)
        elif n < 8:
            emit_w1_chunk(24 + 2 * (n - 6))
            emit_w1_chunk(25 + 2 * (n - 6))
        elif n < 12:
            emit_w1_chunk(28 + (n - 8))
        if 6 <= n < 14:
            emit_w2_pass1(n - 6)
        if n == 11:
            E[("w1wr_done",)] = E[("twr", tseq[0] - 1)]

    BAR = [P.now("t"), P.now("v"), P.now("s")] + QY.done() + QG.done()

    if dev_phase1_only:
        P.op("Y", lambda e: e.nop(), waits=list(BAR))
        return nc, P

    w1_ready = QG.issue(dma(w1t, wt1_scr),
                        waits=[E[("kp", NCH - 1, 7)], E[("vp", NCH - 1, 3)],
                               E[("xkv", NCH - 1)], E[("ame", NCH - 1, 3)],
                               E[("w1wr_done",)]])
    ld_id = QY.issue(dma(idn, id_d), waits=BAR)
    ld_vt2 = QG.issue(dma(vt2, bass.AP(vt_d.tensor, 0, [[16, 128], [128 * 16, 8], [1, 16]])),
                      waits=BAR)
    ld_ut = QG.issue(dma(ut[0:16, :], ut_d), waits=BAR)
    ld_utB = QG.issue(dma(ut[32:48, :], ut_d), waits=BAR)
    ld_fbt = QG.issue(dma(fbt, fbt_d), waits=BAR)
    ld_gate = QY.issue(dma(gate, gate_d), waits=BAR)

    w2_ready = QG.issue(dma(w2t, wt2_scr), waits=list(BAR))
    WTT = w2_ready

    inv_d = 1.0 / D

    def ln(buf_ap, dump_ap, c_m, c_v, c_r, pre_waits):
        P.op("V", E_red(c_m, buf_ap, ALU.add, negate=True), waits=pre_waits, inc="v")
        P.op("V", E_ts(c_m, c_m, inv_d, None, ALU.mult), inc="v")
        P.op("V", E_ts(buf_ap, buf_ap, c_m, None, ALU.add), inc="v")
        P.op("V", E_tt(dump_ap, buf_ap, buf_ap, ALU.mult), inc="v")
        P.op("V", E_red(c_v, dump_ap, ALU.add), inc="v")
        P.op("V", E_ts(c_v, c_v, inv_d, 1e-5, ALU.mult, ALU.add), inc="v")
        evr = P.op("V", E_recip(c_r, c_v), inc="v")
        evs = P.op("S", E_act(c_r, c_r, AF.Sqrt), waits=[evr], inc="s")
        return P.op("V", E_ts(buf_ap, buf_ap, c_r, None, ALU.mult),
                    waits=[evs], inc="v")

    def emit_front_dma(hs):
        j, half = divmod(hs, 2)
        q2 = hs % 2
        wA = [E[("od", 2 * j + 1)], (None if hs < 2 else E[("h1fe", hs - 2)])]
        if hs < 2:
            wA += BAR
        src = bass.AP(scr.tensor, half * 2 * 128 * T + j * D,
                      [[T, 128], [128 * T, 2], [1, D]])
        E[("al", hs)] = QY.issue(dma(att[q2], src), waits=wA)
        for tb in range(2):
            dc = 2 * half + tb
            wX = [(None if hs < 2 else E[("ln2d", hs - 2, tb)])]
            if hs < 2:
                wX += BAR
            src = bass.AP(xres.tensor, (dc * 1024 + j) * D, [[8 * D, 128], [1, D]])
            E[("xl", hs, tb)] = QY.issue(dma(h1[q2][:, tb, :], src), waits=wX)

    def front_steps(hs):
        q2 = hs % 2
        st = {}
        steps = []
        for tb in range(2):
            buf = h1[q2][:, tb, :]
            dump = outf[q2][:, tb, :]
            atb = att[q2][:, tb, :]
            c_m, c_v, c_r = cols[:, 2:3], cols[:, 3:4], cols[:, 4:5]

            def s_pre(tb=tb, buf=buf, atb=atb, c_m=c_m):
                P.op("V", E_tt(buf, buf, atb, ALU.add),
                     waits=[E[("al", hs)], E[("xl", hs, tb)]], inc="v")
                P.op("V", E_red(c_m, buf, ALU.add, negate=True), inc="v")

            def s_mid1(buf=buf, c_m=c_m):
                P.op("V", E_ts(c_m, c_m, inv_d, None, ALU.mult), inc="v")
                P.op("V", E_ts(buf, buf, c_m, None, ALU.add), inc="v")

            def s_mid2(tb=tb, buf=buf, dump=dump, c_v=c_v):
                wdump = [] if hs < 2 else [E[("oD", hs - 2, tb)]]
                P.op("V", E_tt(dump, buf, buf, ALU.mult), waits=wdump, inc="v")
                P.op("V", E_red(c_v, dump, ALU.add), inc="v")

            def s_mid3(tb=tb, c_v=c_v, c_r=c_r):
                P.op("V", E_ts(c_v, c_v, inv_d, 1e-5, ALU.mult, ALU.add), inc="v")
                evr = P.op("V", E_recip(c_r, c_v), inc="v")
                st[tb] = P.op("S", E_act(c_r, c_r, AF.Sqrt), waits=[evr], inc="s")

            def s_fin(tb=tb, buf=buf, atb=atb, c_r=c_r):
                P.op("V", E_ts(buf, buf, c_r, None, ALU.mult),
                     waits=[st[tb]], inc="v")
                evc = P.op("V", E_tt2cp(atb, buf),
                           waits=(list(BAR) if hs < 2 else []), inc="v")
                dst = bass.AP(h1fp[hs % 2].tensor, tb * 128,
                              [[8 * 256, 128], [256, 8], [1, 128]])
                eve = QS.issue((lambda d_, s_: lambda e: e.dma_start_transpose(d_, s_))(
                    dst, atb),
                    waits=[evc, (None if hs < 2 else E[("mm1", hs - 2, 31)])])
                if tb == 1:
                    E[("h1fe", hs)] = eve

            steps += [s_pre, s_mid1, s_mid2, s_mid3, s_fin]
        return steps

    def emit_lr(hs):
        w = [E[("h1fe", hs)], ld_vt2, ev_bankfree[4]]
        for ci in range(8):
            ev = P.op("T", E_mm(pb[4][:16, 0:256], vt2[:, ci, :],
                                h1fp[hs % 2][:, ci, :], ci == 0, ci == 7),
                      waits=(w if ci == 0 else ()), inc=("t" if ci == 7 else None))
        ev1e = P.op("S", E_cp(lr1, pb[4][:16, 0:256]), waits=[ev], inc="s")
        ev_bankfree[4] = ev1e
        ev2m = P.op("T", E_mm(pb[5][0:16, 0:256], fbt, lr1, True, True),
                    waits=[ev1e, ld_fbt, ev_bankfree[5]], inc="t")
        ev2mB = P.op("T", E_mm(pb[5][32:48, 0:256], fbt, lr1, True, True), inc="t")
        ev2e = P.op("S", E_act(lr2[0:16, :], pb[5][0:16, 0:256], AF.Copy,
                               scale=gate[:16, :]),
                    waits=[ev2m, ld_gate], inc="s")
        ev2eB = P.op("S", E_act(lr2[32:48, :], pb[5][32:48, 0:256], AF.Copy,
                                scale=gate[32:48, :]),
                     waits=[ev2mB], inc="s")
        ev_bankfree[5] = ev2eB
        E[("lr2e", hs)] = (ev2e, ev2eB)

    emit_front_dma(0)
    for s in front_steps(0):
        s()
    emit_lr(0)
    FS_SCHED = {3: 0, 5: 1, 7: 2, 9: 3, 10: 4, 11: 5, 12: 6, 13: 7, 14: 8}
    for hs in range(NHS):
        j, half = divmod(hs, 2)
        q2 = hs % 2
        ev2e, ev2eB = E[("lr2e", hs)]
        fsteps = front_steps(hs + 1) if hs + 1 < NHS else None
        for hp in range(16):
            if fsteps is not None:
                if hp == 0:
                    emit_front_dma(hs + 1)
                elif hp in FS_SCHED:
                    fsteps[FS_SCHED[hp]]()
            hcA, hcB = 2 * hp, 2 * hp + 1
            bkA, bkB = hcA % 6, hcB % 6
            for hc, bank in ((hcA, bkA), (hcB, bkB)):
                w = [E[("h1fe", hs)], ev2e, ev2eB, ev_bankfree[bank], ld_ut, ld_utB,
                     (w1_ready if hs == 0 and hc == 0 else None)]
                for cc in range(8):
                    P.op("T", E_mm(pb[bank][:, 0:256],
                                   w1t[:, cc, hc * 128:(hc + 1) * 128],
                                   h1fp[hs % 2][:, cc, :], cc == 0, False),
                         waits=(w if cc == 0 else ()), inc=None)
            evA = P.op("T", E_mmt(pb[bkA][:, 0:256], ut[0:16, hcA * 128:(hcA + 1) * 128],
                                  lr2[0:16, :], False, True, (0, 0)), inc="t")
            evB = P.op("T", E_mmt(pb[bkB][:, 0:256], ut[32:48, hcB * 128:(hcB + 1) * 128],
                                  lr2[32:48, :], False, True, (32, 0)), inc="t")
            E[("mm1", hs, hcA)] = evA
            E[("mm1", hs, hcB)] = evB
            for hc, bank, ev in ((hcA, bkA, evA), (hcB, bkB, evB)):
                rt_free = (E[("sq", hs, hc - 2)] if hc >= 2 else
                           (E[("sq", hs - 1, 30 + hc)] if hs > 0 else None))
                evr = P.op("S", E_act(rtmp[:, hc % 2, :], pb[bank][:, 0:256], AF.Relu),
                           waits=[ev, rt_free], inc="s")
                evq = P.op("V", E_tt(r2[:, hc, :], rtmp[:, hc % 2, :],
                                     rtmp[:, hc % 2, :], ALU.mult),
                           waits=[evr], inc="v")
                E[("sq", hs, hc)] = evq
                ev_bankfree[bank] = evq
        if fsteps is not None:
            fsteps[9]()
        for tb in range(2):
            for csi, cs in enumerate((0, 1)):
                bank = 6 + csi
                w = [E[("sq", hs, 7)], ev_bankfree[bank],
                     (w2_ready if hs == 0 else None)]
                for hc in range(32):
                    mw = (w if hc == 0 else
                          ([E[("sq", hs, hc + 7)]] if hc % 8 == 0 else ()))
                    ev = P.op("T", E_mm(pb[bank], r2[:, hc, tb * 128:(tb + 1) * 128],
                                        w2t[:, hc, cs * 512:(cs + 1) * 512],
                                        hc == 0, hc == 31),
                              waits=mw, inc=("t" if hc == 31 else None))
                E[("mm2", hs, tb, cs)] = ev
                eve = P.op("V", E_tt(outf[q2][:, tb, cs * 512:(cs + 1) * 512], pb[bank],
                                     h1[q2][:, tb, cs * 512:(cs + 1) * 512], ALU.add),
                           waits=[ev, (None if hs < 2 else E[("oD", hs - 2, tb)])], inc="v")
                ev_bankfree[bank] = eve
            if tb == 0 and hs + 1 < NHS:
                emit_lr(hs + 1)
            evl = ln(outf[q2][:, tb, :], h1[q2][:, tb, :],
                     cols[:, 5:6], cols[:, 6:7], cols[:, 7:8], [eve])
            E[("ln2d", hs, tb)] = evl
            dc = 2 * half + tb
            dst = bass.AP(out_d.tensor, (dc * 1024 + j) * D, [[8 * D, 128], [1, D]])
            E[("oD", hs, tb)] = QY.issue(dma(dst, outf[q2][:, tb, :]), waits=[evl])

    P.op("Y", lambda e: e.nop(), waits=QY.done() + QG.done())
    return nc, P


def _finish(nc, P):
    sems = {}
    import contextlib
    stack = contextlib.ExitStack()
    names = ["t", "v", "s", "ts0"] + [f"g{i}" for i in range(4)] + [f"y{i}" for i in range(6)]
    for name in names:
        sems[name] = stack.enter_context(nc.semaphore(name))
    blk = stack.enter_context(nc.Block())
    blk.tensor(lambda e: P.emit("T", e, sems))
    blk.vector(lambda e: P.emit("V", e, sems))
    blk.scalar(lambda e: P.emit("S", e, sems))
    blk.sync(lambda e: P.emit("Y", e, sems))
    blk.gpsimd(lambda e: P.emit("G", e, sems))
    stack.close()
    return nc


def host_inputs(X, W_Q, W_K, W_V, w_fc, w_proj, u, v, fast_b, gate):
    f32 = np.float32
    import ml_dtypes
    i = np.arange(CH, dtype=np.float64)
    g = np.float64(GAMMA)
    q64, k64 = g ** i, g ** (-i)
    qrow, krow = q64.astype(f32), k64.astype(f32)
    qlo = (q64 - qrow.astype(np.float64)).astype(f32)
    klo = (k64 - krow.astype(np.float64)).astype(f32)
    rep = lambda r: np.ascontiguousarray(np.broadcast_to(r, (128, CH)))
    p = np.arange(128)[:, None, None]
    sc = np.arange(4)[None, :, None]
    q = np.arange(CH)[None, None, :]
    mask = (q >= sc * 128 + p).astype(ml_dtypes.bfloat16)
    common = {
        "wq": np.ascontiguousarray(W_Q, f32),
        "wk": np.ascontiguousarray(W_K, f32),
        "wfc": np.ascontiguousarray(w_fc, f32),
        "wpj": np.ascontiguousarray(w_proj, f32),
        "vt_in": np.ascontiguousarray(v.T, f32),
        "ut_in": np.ascontiguousarray(u.T, f32),
        "fbt_in": np.ascontiguousarray(fast_b.T, f32),
        "gate_in": np.ascontiguousarray(np.broadcast_to(
            np.asarray(gate, f32).reshape(1, 1), (128, 1))),
        "mask_in": np.ascontiguousarray(mask),
        "qscale": rep(qrow), "kscale": rep(krow),
        "qscale_lo": rep(qlo), "kscale_lo": rep(klo),
        "ident": np.eye(128, dtype=f32),
        "igc_in": np.eye(128, dtype=f32),
    }
    in_maps = []
    for k in range(8):
        b, h = k // 2, k % 2
        m = dict(common)
        m["xfm"] = np.ascontiguousarray(X[b].T.astype(f32, copy=False))
        m["xres"] = np.ascontiguousarray(X[b, h * 4096:(h + 1) * 4096].astype(f32, copy=False))
        m["wv"] = np.ascontiguousarray(W_V[:, h * DH:(h + 1) * DH], f32)
        in_maps.append(m)
    return in_maps


def kernel(L0_attn_X, L0_attn_att_mask, L0_attn_S_n, L0_attn_W_Q, L0_attn_W_K,
           L0_attn_W_V, L0_ffn_fast_b, L0_ffn_w_fc, L0_ffn_w_proj, L0_ffn_u,
           L0_ffn_v, L0_ffn_gate):
    X = np.asarray(L0_attn_X)
    in_maps = host_inputs(X, np.asarray(L0_attn_W_Q), np.asarray(L0_attn_W_K),
                          np.asarray(L0_attn_W_V), np.asarray(L0_ffn_w_fc),
                          np.asarray(L0_ffn_w_proj), np.asarray(L0_ffn_u),
                          np.asarray(L0_ffn_v), np.asarray(L0_ffn_fast_b),
                          np.asarray(L0_ffn_gate))
    nc, P = build_nc()
    nc = _finish(nc, P)
    res = run_bass_kernel_spmd(nc, in_maps, core_ids=list(range(8)))
    y = np.empty((B, T, D), dtype=np.float32)
    for k in range(8):
        b, h = k // 2, k % 2
        y[b, h * 4096:(h + 1) * 4096] = res.results[k]["out"]
    return y
